# version 01 (final)
# baseline (speedup 1.0000x reference)
import math
from contextlib import ExitStack
import numpy as np
import ml_dtypes
import concourse.bass as bass
import concourse.mybir as mybir
from concourse.bass_utils import run_bass_kernel_spmd

F32 = mybir.dt.float32
BF16 = mybir.dt.bfloat16
F16 = mybir.dt.float16
AF = mybir.ActivationFunctionType
AX = mybir.AxisListType
ALU = mybir.AluOpType
BF = ml_dtypes.bfloat16

B, S, E = 2, 4096, 4096
H0, HD0, ROT = 16, 256, 64
H1, HD1 = 32, 128
NC = 8
HALF = S // 2
SUF = S // 2
MASK_NEG = -1.0e9
EXP_SHIFT = 32.0
SC0 = 1.0 / 16.0
SC1 = 1.0 / math.sqrt(128.0)
RACE_CHECK = [True]


class SelfSync:

    def __init__(self, eng, sem_handle):
        self.eng, self.sem, self.n = eng, sem_handle, 0

    def step(self, ins):
        ins.then_inc(self.sem, 1)
        self.n += 1

    def barrier(self):
        self.eng.wait_ge(self.sem, self.n)

def _mknc(scratch=16384):
    return bass.Bass("TRN2", target_bir_lowering=False,
                     dynamic_dma_scratch_size=scratch,
                     detect_race_conditions=RACE_CHECK[0])



def _tile_lhsT(W_do):
    d, O = W_do.shape
    assert d == 4096 and O % 128 == 0
    return np.ascontiguousarray(
        W_do.reshape(32, 128, O // 128, 128).transpose(2, 1, 0, 3))


def stage_inputs(I):
    x0 = I["L0_attn_hidden_states"]
    x0T = [np.ascontiguousarray(x0[b].T) for b in range(B)]

    inv_freq = (1.0 / 10000.0 ** (np.arange(0, ROT, 2, dtype=np.float32) / ROT)
                ).astype(np.float32)
    t = np.arange(S, dtype=np.float32)
    sinu = np.einsum("i,j->ij", t, inv_freq).astype(np.float32)
    sin_fm = np.ascontiguousarray(np.repeat(np.sin(sinu).astype(np.float32), 2, axis=1).T)
    cos_fm = np.ascontiguousarray(np.repeat(np.cos(sinu).astype(np.float32), 2, axis=1).T)

    P64 = np.zeros((64, 64), dtype=BF)
    for m in range(64):
        P64[m ^ 1, m] = BF(1.0)

    ident = np.eye(128, dtype=BF)
    ones_bf = np.ones((128, 1), dtype=BF)
    ones_f32 = np.ones((1, 128), dtype=np.float32)

    kk = np.arange(128)[:, None]
    qq = np.arange(512)[None, :]
    maskp = np.stack([np.where(c * 128 + kk <= qq, 0.0, MASK_NEG)
                      for c in range(4)]).astype(np.float32)

    wq, wk, wv = I["L0_attn_Wq"], I["L0_attn_Wk"], I["L0_attn_Wv"]
    qkv_t = []
    for par in range(2):
        hs = slice(par * 8 * HD0, (par + 1) * 8 * HD0)
        Wc = np.concatenate([wq[:, hs], wk[:, hs], wv[:, hs]], axis=1)
        qkv_t.append(_tile_lhsT(Wc))
    wo_t = _tile_lhsT(I["L0_attn_Wo"])

    route_rhs = np.ascontiguousarray(
        I["L0_ffn_route_w"].T.reshape(32, 128, 8).transpose(1, 0, 2))
    A_rhs = np.ascontiguousarray(
        I["L0_ffn_A"].transpose(2, 0, 1).reshape(32, 128, 128)
        .transpose(1, 0, 2))
    Bw_lhsT = np.ascontiguousarray(
        I["L0_ffn_Bw"].transpose(0, 2, 1).reshape(128, 4096))

    wq1, wq1b = I["L1_attn_wq_w"], I["L1_attn_wq_b"]
    wq1_t, wq1_b = [], []
    for par in range(2):
        hs = slice(par * 16 * HD1, (par + 1) * 16 * HD1)
        wq1_t.append(_tile_lhsT(np.ascontiguousarray(wq1[hs].T)))
        wq1_b.append(np.ascontiguousarray(wq1b[hs].reshape(16, 128).T))
    wk1_t = _tile_lhsT(np.ascontiguousarray(I["L1_attn_wk_w"].T))
    wv1_t = _tile_lhsT(np.ascontiguousarray(I["L1_attn_wv_w"].T))
    wk1_b = np.ascontiguousarray(I["L1_attn_wk_b"].reshape(1, 128))
    wv1_b = np.ascontiguousarray(I["L1_attn_wv_b"].reshape(1, 128))
    wo1_t = _tile_lhsT(np.ascontiguousarray(I["L1_attn_wo_w"].T))
    wo1_b = np.ascontiguousarray(I["L1_attn_wo_b"].reshape(32, 128).T)

    fw = I["L1_ffn_weight"]
    fwcb = np.ascontiguousarray(
        fw.T.reshape(32, 128, 32, 128).transpose(2, 1, 0, 3))
    fs = I["L1_ffn_scales"]
    fscb = np.ascontiguousarray(np.broadcast_to(
        fs.reshape(32, 128, 32).transpose(0, 2, 1)[:, None, :, :],
        (32, 128, 32, 128)))

    common = dict(cos_fm=cos_fm, sin_fm=sin_fm, P64=P64, ident=ident,
                  ones_bf=ones_bf, ones_f32=ones_f32, maskp=maskp,
                  wo_t=wo_t, route_rhs=route_rhs, A_rhs=A_rhs, Bw_lhsT=Bw_lhsT,
                  wk1_t=wk1_t, wv1_t=wv1_t, wk1_b=wk1_b, wv1_b=wv1_b,
                  wo1_t=wo1_t, wo1_b=wo1_b, fwcb=fwcb, fscb=fscb)

    cores = []
    for c in range(NC):
        seq, par = c // 4, c % 2
        cores.append(dict(common, x0T=x0T[seq], qkv_t=qkv_t[par],
                          wq1_t=wq1_t[par], wq1_b=wq1_b[par]))
    return cores


def core_geometry(c):
    seq, ptype, par = c // 4, (c // 2) % 2, c % 2
    half = (par * HALF, (par + 1) * HALF)
    blocks = (0, 3) if ptype == 0 else (1, 2)
    myblk = blocks[par]
    return seq, ptype, par, half, blocks, myblk



OSUP = 6
CB_PER_SUP = 8
NTB = 8


def build_phaseA():
    nc = _mknc()
    x0T = nc.dram_tensor("x0T", [4096, 4096], F32, kind="ExternalInput").ap()
    qkv_t = nc.dram_tensor("qkv_t", [48, 128, 32, 128], F32, kind="ExternalInput").ap()
    cos_fm = nc.dram_tensor("cos_fm", [64, 4096], F32, kind="ExternalInput").ap()
    sin_fm = nc.dram_tensor("sin_fm", [64, 4096], F32, kind="ExternalInput").ap()
    P64 = nc.dram_tensor("P64", [64, 64], BF16, kind="ExternalInput").ap()
    qkvT = nc.dram_tensor("qkvT", [6144, 4096], BF16, kind="ExternalOutput").ap()
    x0bf = nc.dram_tensor("x0bf", [4096, 4096], BF16, kind="Internal").ap()

    NPS = 4
    NEV = 4
    rot_cbs = {2 * h for h in range(8)} | {16 + 2 * h for h in range(8)}

    with ExitStack() as ctx:
        sb = lambda n, sh, dt: ctx.enter_context(nc.sbuf_tensor(n, sh, dt))
        pst = lambda n, sh: ctx.enter_context(nc.psum_tensor(n, sh, F32))
        sem = lambda n: ctx.enter_context(nc.semaphore(n))
        wsb = sb("wsb", [128, CB_PER_SUP * 32 * 128], BF16)
        xsb = sb("xsb", [128, 2 * 32 * 512], BF16)
        evac = sb("evac", [128, NEV * 512], BF16)
        rotq = sb("rotq", [64, 4096], BF16)
        rota = sb("rota", [64, 512], F32)
        rotb = sb("rotb", [64, 512], F32)
        cossb = sb("cossb", [64, 4096], F32)
        sinsb = sb("sinsb", [64, 4096], F32)
        p64sb = sb("p64sb", [64, 64], BF16)
        ps = pst("ps", [128, NPS * 512])
        swp = pst("swp", [64, 512])
        s_w = [sem("s_w0"), sem("s_w1")]
        s_x = [sem("s_x0"), sem("s_x1")]
        s_mm = sem("s_mm")
        s_ev = sem("s_ev")
        s_out = [sem(f"s_out{i}") for i in range(4)]
        s_pre = sem("s_pre")
        s_tab = sem("s_tab")
        s_rl = sem("s_rl")
        s_rs = sem("s_rs")
        s_rv = sem("s_rv")
        s_ro = sem("s_ro")
        s_vs = sem("s_vs")
        block = ctx.enter_context(nc.Block())
        ntiles = OSUP * NTB * CB_PER_SUP

        @block.gpsimd
        def _(g):
            for i in range(8):
                g.dma_start(out=x0bf[i * 512:(i + 1) * 512, :],
                            in_=x0T[i * 512:(i + 1) * 512, :]).then_inc(s_pre, 16)
            g.dma_start(out=cossb[:], in_=cos_fm[:]).then_inc(s_tab, 16)
            g.dma_start(out=sinsb[:], in_=sin_fm[:]).then_inc(s_tab, 16)
            g.dma_start(out=p64sb[:], in_=P64[:]).then_inc(s_tab, 16)
            for sup in range(OSUP):
                if sup >= 1:
                    g.wait_ge(s_ev, sup * NTB * CB_PER_SUP)
                for j in range(CB_PER_SUP):
                    cb = sup * CB_PER_SUP + j
                    g.dma_start(out=wsb[:, j * 4096:(j + 1) * 4096],
                                in_=qkv_t[cb].rearrange("p a b -> p (a b)")
                                ).then_inc(s_w[sup % 2], 16)

        @block.sync
        def _(s):
            s.wait_ge(s_pre, 16 * 8)
            for sup in range(OSUP):
                for tb in range(NTB):
                    it = sup * NTB + tb
                    buf = it % 2
                    if it >= 2:
                        s.wait_ge(s_ev, (it - 1) * CB_PER_SUP)
                    for dc in range(32):
                        s.dma_start(
                            out=xsb[:, (buf * 32 + dc) * 512:(buf * 32 + dc + 1) * 512],
                            in_=x0bf[dc * 128:(dc + 1) * 128, tb * 512:(tb + 1) * 512],
                        ).then_inc(s_x[it % 2], 16)

        @block.tensor
        def _(tns):
            idx = 0
            for sup in range(OSUP):
                for tb in range(NTB):
                    it = sup * NTB + tb
                    buf = it % 2
                    tns.wait_ge(s_x[it % 2], 16 * 32 * (it // 2 + 1))
                    if tb == 0:
                        tns.wait_ge(s_w[sup % 2], 16 * CB_PER_SUP * (sup // 2 + 1))
                    for j in range(CB_PER_SUP):
                        cb = sup * CB_PER_SUP + j
                        bank = idx % NPS
                        if idx >= NPS:
                            tns.wait_ge(s_ev, idx - NPS + 1)
                        for dc in range(32):
                            ins = tns.matmul(
                                ps[:, bank * 512:(bank + 1) * 512],
                                wsb[:, j * 4096 + dc * 128: j * 4096 + (dc + 1) * 128],
                                xsb[:, (buf * 32 + dc) * 512:(buf * 32 + dc + 1) * 512],
                                start=(dc == 0), stop=(dc == 31))
                        ins.then_inc(s_mm, 1)
                        idx += 1
            tns.wait_ge(s_tab, 16 * 3)
            for r in range(16):
                tns.wait_ge(s_rl, 16 * (r + 1))
                for nt in range(8):
                    if r * 8 + nt >= 1:
                        tns.wait_ge(s_rv, r * 8 + nt)
                    tns.matmul(swp[:, :], p64sb[:],
                               rotq[:, nt * 512:(nt + 1) * 512],
                               start=True, stop=True).then_inc(s_rs, 1)

        @block.vector
        def _(v):
            ss = SelfSync(v, s_vs)
            for idx in range(ntiles):
                v.wait_ge(s_mm, idx + 1)
                ev = idx % NEV
                if idx >= NEV:
                    v.wait_ge(s_out[idx % NEV], 16 * (idx // NEV))
                bank = idx % NPS
                v.tensor_copy(evac[:, ev * 512:(ev + 1) * 512],
                              ps[:, bank * 512:(bank + 1) * 512]).then_inc(s_ev, 1)
            for r in range(16):
                for nt in range(8):
                    i = r * 8 + nt
                    sl = slice(nt * 512, (nt + 1) * 512)
                    v.wait_ge(s_rs, i + 1)
                    v.tensor_tensor(rota[:, :], rotq[:, sl], cossb[:, sl], ALU.mult)
                    ss.step(v.tensor_tensor(rotb[:, :], swp[:, :], sinsb[:, sl],
                                            ALU.mult))
                    ss.barrier()
                    v.tensor_tensor(rotq[:, sl], rota[:, :], rotb[:, :],
                                    ALU.add).then_inc(s_rv, 1)

        @block.scalar
        def _(sc):
            for idx in range(ntiles):
                sup, rem = divmod(idx, NTB * CB_PER_SUP)
                tb, j = divmod(rem, CB_PER_SUP)
                cb = sup * CB_PER_SUP + j
                ev = idx % NEV
                sc.wait_ge(s_ev, idx + 1)
                sc.dma_start(out=qkvT[cb * 128:(cb + 1) * 128,
                                      tb * 512:(tb + 1) * 512],
                             in_=evac[:, ev * 512:(ev + 1) * 512]
                             ).then_inc(s_out[idx % NEV], 16)
            for i in range(4):
                sc.wait_ge(s_out[i], 16 * (ntiles // 4))
            for r in range(16):
                h, qk = r % 8, r // 8
                row0 = qk * 2048 + h * 256
                if r >= 1:
                    sc.wait_ge(s_ro, 16 * r)
                sc.dma_start(out=rotq[:, :],
                             in_=qkvT[row0:row0 + 64, :]).then_inc(s_rl, 16)
                sc.wait_ge(s_rv, (r + 1) * 8)
                sc.dma_start(out=qkvT[row0:row0 + 64, :],
                             in_=rotq[:, :]).then_inc(s_ro, 16)
            sc.wait_ge(s_ro, 16 * 16)

    return nc






CUM_G = [0, 4, 12, 24, 40, 60, 84, 112, 144]


def build_phaseB(NH=8):
    nc = _mknc()
    qkvT = nc.dram_tensor("qkvT", [6144, 4096], BF16, kind="ExternalInput").ap()
    maskp = nc.dram_tensor("maskp", [4, 128, 512], F32, kind="ExternalInput").ap()
    identt = nc.dram_tensor("identt", [128, 128], BF16, kind="ExternalInput").ap()
    onesb = nc.dram_tensor("onesb", [128, 1], BF16, kind="ExternalInput").ap()
    onesf = nc.dram_tensor("onesf", [1, 128], F32, kind="ExternalInput").ap()
    attnT = nc.dram_tensor("attnT", [2048, 4096], BF16, kind="ExternalOutput").ap()

    NG = 8
    CPH = CUM_G[NG]

    with ExitStack() as ctx:
        sb = lambda n, sh, dt: ctx.enter_context(nc.sbuf_tensor(n, sh, dt))
        sem = lambda n: ctx.enter_context(nc.semaphore(n))
        kTs = sb("kTs", [128, 2 * 2 * 4096], BF16)
        qTs = sb("qTs", [128, 2 * 2 * 4096], BF16)
        vTs = sb("vTs", [128, 2 * 2 * 4096], BF16)
        vtok = sb("vtok", [128, 2 * 32 * 256], BF16)
        ptile = sb("ptile", [128, 3 * 512], BF16)
        masks = sb("masks", [128, 4 * 512], F32)
        idsb = sb("idsb", [128, 128], BF16)
        onbs = sb("onbs", [128, 1], BF16)
        onfs = sb("onfs", [1, 128], F32)
        rl = sb("rl", [1, 512], F32)
        nshift = sb("nshift", [128, 1], F32)
        rlb = sb("rlb", [128, 512], F32)
        attns = sb("attns", [128, 2 * 2 * 512], BF16)
        spsum = ctx.enter_context(nc.psum_tensor("spsum", [128, 2 * 512], F32))
        opsum = ctx.enter_context(nc.psum_tensor("opsum", [128, 2 * 512], F32))
        lpsum = ctx.enter_context(nc.psum_tensor("lpsum", [1, 512], F32))
        bpsum = ctx.enter_context(nc.psum_tensor("bpsum", [128, 512], F32))
        vtp = [ctx.enter_context(nc.psum_tensor("vtp0", [128, 256], BF16)),
               ctx.enter_context(nc.psum_tensor("vtp1", [128, 256], BF16))]
        s_pre = sem("s_pre")
        s_ld = [sem("s_ld0"), sem("s_ld1")]
        s_vt = sem("s_vt")
        s_vc = sem("s_vc")
        s_s = sem("s_s")
        s_mk = sem("s_mk")
        s_p = sem("s_p")
        s_pc = sem("s_pc")
        s_rcp = sem("s_rcp")
        s_bcm = sem("s_bcm")
        s_oev = sem("s_oev")
        s_ao = [sem("s_ao0"), sem("s_ao1")]
        s_ns = sem("s_ns")
        s_vs = sem("s_vs")
        block = ctx.enter_context(nc.Block())

        def chunk_idx(h, g, kc):
            return h * CPH + CUM_G[g] + kc

        def diag_idx(h, g, kc):
            return (h * NG + g) * 4 + (kc - (4 * (g + 1) - 4))

        @block.sync
        def _(s):
            for rel in range(4):
                s.dma_start(out=masks[:, rel * 512:(rel + 1) * 512],
                            in_=maskp[rel]).then_inc(s_pre, 16)
            for dst, src in [(idsb[:, :], identt[:, :]),
                             (onbs[:, :], onesb[:, :]),
                             (onfs[:, :], onesf[:, :])]:
                s.dma_start(out=dst, in_=src).then_inc(s_pre, 16)
            for h in range(NH):
                buf = h % 2
                if h >= 2:
                    s.wait_ge(s_pc, CPH * (h - 1))
                for base, dst in ((2048, kTs), (0, qTs), (4096, vTs)):
                    for hc in range(2):
                        s.dma_start(
                            out=dst[:, (buf * 2 + hc) * 4096:(buf * 2 + hc + 1) * 4096],
                            in_=qkvT[base + h * 256 + hc * 128:
                                     base + h * 256 + (hc + 1) * 128, :]
                        ).then_inc(s_ld[h % 2], 16)

        @block.tensor
        def _(tns):
            tns.wait_ge(s_pre, 112)
            for h in range(NH):
                buf = h % 2
                tns.wait_ge(s_ld[h % 2], 96 * (h // 2 + 1))
                for kc in range(32):
                    gkc = 32 * h + kc
                    if gkc >= 2:
                        tns.wait_ge(s_vc, gkc - 1)
                    for hc in range(2):
                        ins = tns.transpose(
                            vtp[kc % 2][:, hc * 128:(hc + 1) * 128],
                            vTs[:, (buf * 2 + hc) * 4096 + kc * 128:
                                (buf * 2 + hc) * 4096 + (kc + 1) * 128],
                            idsb[:, :])
                    ins.then_inc(s_vt, 1)
                tns.wait_ge(s_vc, 32 * (h + 1))
                for g in range(NG):
                    j = h * NG + g
                    nk = 4 * (g + 1)

                    def sum_pv(kc, h=h, g=g, buf=buf, nk=nk, j=j):
                        i = chunk_idx(h, g, kc)
                        tns.wait_ge(s_p, i + 1)
                        if kc == 0:
                            tns.wait_ge(s_oev, 2 * j)
                        pt = ptile[:, (i % 3) * 512:(i % 3 + 1) * 512]
                        tns.matmul(lpsum[:, :], onbs[:, :], pt,
                                   start=(kc == 0), stop=(kc == nk - 1))
                        for hc in range(2):
                            ins = tns.matmul(
                                opsum[:, hc * 512:(hc + 1) * 512],
                                vtok[:, buf * 32 * 256 + kc * 256 + hc * 128:
                                     buf * 32 * 256 + kc * 256 + (hc + 1) * 128],
                                pt, start=(kc == 0), stop=(kc == nk - 1))
                        ins.then_inc(s_pc, 1)

                    for kc in range(nk):
                        i = chunk_idx(h, g, kc)
                        bank = i % 2
                        if i >= 2:
                            tns.wait_ge(s_p, i - 1)
                        for hc in range(2):
                            ins = tns.matmul(
                                spsum[:, bank * 512:(bank + 1) * 512],
                                kTs[:, (buf * 2 + hc) * 4096 + kc * 128:
                                    (buf * 2 + hc) * 4096 + (kc + 1) * 128],
                                qTs[:, (buf * 2 + hc) * 4096 + g * 512:
                                    (buf * 2 + hc) * 4096 + (g + 1) * 512],
                                start=(hc == 0), stop=(hc == 1))
                        ins.then_inc(s_s, 1)
                        if kc >= 1:
                            sum_pv(kc - 1)
                    sum_pv(nk - 1)
                    tns.wait_ge(s_rcp, j + 1)
                    tns.matmul(bpsum[:, :], onfs[:, :], rl[:, :],
                               start=True, stop=True).then_inc(s_bcm, 1)

        @block.vector
        def _(v):
            ss = SelfSync(v, s_vs)
            v.memset(nshift[:, :], -EXP_SHIFT).then_inc(s_ns, 1)
            for h in range(NH):
                for kc in range(32):
                    v.wait_ge(s_vt, 32 * h + kc + 1)
                    v.tensor_copy(
                        vtok[:, (h % 2) * 32 * 256 + kc * 256:
                             (h % 2) * 32 * 256 + (kc + 1) * 256],
                        vtp[kc % 2][:, :]).then_inc(s_vc, 1)
                for g in range(NG):
                    j = h * NG + g
                    nk = 4 * (g + 1)
                    for kc in range(nk - 4, nk):
                        i = chunk_idx(h, g, kc)
                        rel = kc - (nk - 4)
                        v.wait_ge(s_s, i + 1)
                        v.tensor_tensor(
                            spsum[:, (i % 2) * 512:(i % 2 + 1) * 512],
                            spsum[:, (i % 2) * 512:(i % 2 + 1) * 512],
                            masks[:, rel * 512:(rel + 1) * 512],
                            ALU.add).then_inc(s_mk, 1)
                    v.wait_ge(s_pc, chunk_idx(h, g, nk - 1) + 1)
                    v.reciprocal(rl[:, :], lpsum[:, :]).then_inc(s_rcp, 1)
                    v.wait_ge(s_bcm, j + 1)
                    ss.step(v.tensor_copy(rlb[:, :], bpsum[:, :]))
                    ss.barrier()
                    st = j % 2
                    if j >= 2:
                        v.wait_ge(s_ao[j % 2], 32 * (j // 2))
                    for hc in range(2):
                        v.scalar_tensor_tensor(
                            attns[:, (st * 2 + hc) * 512:(st * 2 + hc + 1) * 512],
                            opsum[:, hc * 512:(hc + 1) * 512],
                            1.0, rlb[:, :], ALU.mult, ALU.mult).then_inc(s_oev, 1)

        @block.scalar
        def _(sc):
            sc.wait_ge(s_pre, 112)
            sc.wait_ge(s_ns, 1)
            for h in range(NH):
                for g in range(NG):
                    j = h * NG + g
                    nk = 4 * (g + 1)
                    for kc in range(nk):
                        i = chunk_idx(h, g, kc)
                        if i >= 3:
                            sc.wait_ge(s_pc, i - 2)
                        if kc >= nk - 4:
                            sc.wait_ge(s_mk, diag_idx(h, g, kc) + 1)
                        else:
                            sc.wait_ge(s_s, i + 1)
                        sc.activation(ptile[:, (i % 3) * 512:(i % 3 + 1) * 512],
                                      spsum[:, (i % 2) * 512:(i % 2 + 1) * 512],
                                      AF.Exp, bias=nshift[:, 0:1],
                                      scale=SC0).then_inc(s_p, 1)
                    sc.wait_ge(s_oev, 2 * (j + 1))
                    st = j % 2
                    for hc in range(2):
                        sc.dma_start(
                            out=attnT[h * 256 + hc * 128:h * 256 + (hc + 1) * 128,
                                      g * 512:(g + 1) * 512],
                            in_=attns[:, (st * 2 + hc) * 512:(st * 2 + hc + 1) * 512]
                        ).then_inc(s_ao[j % 2], 16)
            sc.wait_ge(s_ao[0], 32 * NH * NG // 2)
            sc.wait_ge(s_ao[1], 32 * NH * NG // 2)

    return nc



def build_phaseC():
    nc = _mknc()
    attn_my = nc.dram_tensor("attn_my", [4096, 2048], BF16, kind="ExternalInput").ap()
    x0Th = nc.dram_tensor("x0Th", [4096, 2048], F32, kind="ExternalInput").ap()
    wo_t = nc.dram_tensor("wo_t", [32, 128, 32, 128], F32, kind="ExternalInput").ap()
    route_rhs = nc.dram_tensor("route_rhs", [128, 32, 8], F32, kind="ExternalInput").ap()
    A_rhs = nc.dram_tensor("A_rhs", [128, 32, 128], F32, kind="ExternalInput").ap()
    Bw_lhsT = nc.dram_tensor("Bw_lhsT", [128, 4096], F32, kind="ExternalInput").ap()
    wk1_t = nc.dram_tensor("wk1_t", [1, 128, 32, 128], F32, kind="ExternalInput").ap()
    wv1_t = nc.dram_tensor("wv1_t", [1, 128, 32, 128], F32, kind="ExternalInput").ap()
    wk1_b = nc.dram_tensor("wk1_b", [1, 128], F32, kind="ExternalInput").ap()
    wv1_b = nc.dram_tensor("wv1_b", [1, 128], F32, kind="ExternalInput").ap()
    identt = nc.dram_tensor("identt", [128, 128], BF16, kind="ExternalInput").ap()
    onesb = nc.dram_tensor("onesb", [128, 1], BF16, kind="ExternalInput").ap()
    onesf = nc.dram_tensor("onesf", [1, 128], F32, kind="ExternalInput").ap()
    x1T_bf = nc.dram_tensor("x1T_bf", [4096, 2048], BF16, kind="ExternalOutput").ap()
    x2T_f32 = nc.dram_tensor("x2T_f32", [4096, 2048], F32, kind="ExternalOutput").ap()
    x2T_bf = nc.dram_tensor("x2T_bf", [4096, 2048], BF16, kind="ExternalOutput").ap()
    k1T_h = nc.dram_tensor("k1T_h", [128, 2048], BF16, kind="ExternalOutput").ap()
    v1tok_h = nc.dram_tensor("v1tok_h", [2048, 128], BF16, kind="ExternalOutput").ap()
    wo_bf = nc.dram_tensor("wo_bf", [32, 128, 4096], BF16, kind="Internal").ap()

    NTS = 4

    with ExitStack() as ctx:
        sb = lambda n, sh, dt: ctx.enter_context(nc.sbuf_tensor(n, sh, dt))
        sem = lambda n: ctx.enter_context(nc.semaphore(n))
        atn = sb("atn", [128, 32 * 512], BF16)
        zf = sb("zf", [128, 32 * 512], F32)
        zbf = sb("zbf", [128, 32 * 512], BF16)
        wos = sb("wos", [128, 2 * 4096], BF16)
        x0s = sb("x0s", [128, 4 * 512], F32)
        asb = sb("asb", [128, 32 * 128], BF16)
        bwsb = sb("bwsb", [128, 4096], BF16)
        rtsb = sb("rtsb", [128, 32 * 8], BF16)
        wksb = sb("wksb", [128, 32 * 128], BF16)
        wvsb = sb("wvsb", [128, 32 * 128], BF16)
        kbsb = sb("kbsb", [128, 1], F32)
        vbsb = sb("vbsb", [128, 1], F32)
        idsb = sb("idsb", [128, 128], BF16)
        onbs = sb("onbs", [128, 1], BF16)
        onfs = sb("onfs", [1, 128], F32)
        z2s = sb("z2s", [128, 2 * 512], BF16)
        st1 = sb("st1", [1, 512], F32)
        st2 = sb("st2", [1, 512], F32)
        mean = sb("mean", [1, 512], F32)
        varv = sb("varv", [1, 512], F32)
        nmean = sb("nmean", [1, 512], F32)
        rstd = sb("rstd", [1, 512], F32)
        epsb = sb("epsb", [1, 1], F32)
        nmb = sb("nmb", [128, 512], F32)
        rsb = sb("rsb", [128, 512], F32)
        lgs = sb("lgs", [128, 8], F32)
        mx = sb("mx", [128, 1], F32)
        ge01 = sb("ge01", [128, 8], F32)
        rw = sb("rw", [128, 8], F32)
        whbf = sb("whbf", [128, 128], BF16)
        whT = sb("whT", [128, 4 * 128], BF16)
        kvs = sb("kvs", [128, 2 * 512], BF16)
        vts = sb("vts", [128, 4 * 128], BF16)
        mps = ctx.enter_context(nc.psum_tensor("mps", [128, 2 * 512], F32))
        st1p = ctx.enter_context(nc.psum_tensor("st1p", [1, 512], F32))
        st2p = ctx.enter_context(nc.psum_tensor("st2p", [1, 512], F32))
        bps = ctx.enter_context(nc.psum_tensor("bps", [128, 512], F32))
        lps = ctx.enter_context(nc.psum_tensor("lps", [128, 8], F32))
        hps = ctx.enter_context(nc.psum_tensor("hps", [128, 128], F32))
        tp0 = ctx.enter_context(nc.psum_tensor("tp0", [128, 128], BF16))

        s_cst = sem("s_cst")
        s_wc = sem("s_wc")
        s_at = sem("s_at")
        s_x0 = [sem(f"s_x0{i}") for i in range(4)]
        s_mw = [sem("s_mw0"), sem("s_mw1")]
        s_mm = sem("s_mm")
        s_zev = sem("s_zev")
        s_sq = sem("s_sq")
        s_st = sem("s_st")
        s_l1 = sem("s_l1")
        s_l2 = sem("s_l2")
        s_bc = sem("s_bc")
        s_bv = sem("s_bv")
        s_nr = sem("s_nr")
        s_x1o = sem("s_x1o")
        s_fl = sem("s_fl")
        s_fr = sem("s_fr")
        s_fh = sem("s_fh")
        s_fw = sem("s_fw")
        s_ft = sem("s_ft")
        s_fc = sem("s_fc")
        s_x2o = sem("s_x2o")
        s_kv = sem("s_kv")
        s_ke = sem("s_ke")
        s_vt = sem("s_vt")
        s_vc = sem("s_vc")
        s_ko = sem("s_ko")
        s_vs = sem("s_vs")
        s_as = sem("s_as")
        block = ctx.enter_context(nc.Block())

        @block.gpsimd
        def _(g):
            g.dma_start(out=asb[:], in_=A_rhs.rearrange("p a b -> p (a b)")).then_inc(s_cst, 16)
            g.dma_start(out=bwsb[:], in_=Bw_lhsT[:, :]).then_inc(s_cst, 16)
            g.dma_start(out=rtsb[:], in_=route_rhs.rearrange("p a b -> p (a b)")).then_inc(s_cst, 16)
            g.dma_start(out=wksb[:], in_=wk1_t[0].rearrange("p a b -> p (a b)")).then_inc(s_cst, 16)
            g.dma_start(out=wvsb[:], in_=wv1_t[0].rearrange("p a b -> p (a b)")).then_inc(s_cst, 16)
            g.dma_start(out=kbsb[:],
                        in_=wk1_b.rearrange("a b -> b a")).then_inc(s_cst, 16)
            g.dma_start(out=vbsb[:], in_=wv1_b.rearrange("a b -> b a")).then_inc(s_cst, 16)
            g.dma_start(out=idsb[:], in_=identt[:, :]).then_inc(s_cst, 16)
            g.dma_start(out=onbs[:], in_=onesb[:, :]).then_inc(s_cst, 16)
            g.dma_start(out=onfs[:], in_=onesf[:, :]).then_inc(s_cst, 16)
            for cb in range(32):
                g.dma_start(out=wo_bf[cb],
                            in_=wo_t[cb].rearrange("p a b -> p (a b)")).then_inc(s_wc, 16)

        @block.sync
        def _(s):
            for ts in range(NTS):
                if ts >= 1:
                    s.wait_ge(s_mm, ts * 64)
                for dc in range(32):
                    s.dma_start(out=atn[:, dc * 512:(dc + 1) * 512],
                                in_=attn_my[dc * 128:(dc + 1) * 128,
                                            ts * 512:(ts + 1) * 512]).then_inc(s_at, 16)
                for oc in range(32):
                    g2 = ts * 32 + oc
                    s.wait_ge(s_wc, 16 * (oc + 1))
                    if g2 >= 2:
                        tsp, ocp = divmod(g2 - 2, 32)
                        s.wait_ge(s_mm, tsp * 64 + ocp + 1)
                    s.dma_start(out=wos[:, (g2 % 2) * 4096:(g2 % 2 + 1) * 4096],
                                in_=wo_bf[oc]).then_inc(s_mw[g2 % 2], 16)
                    gx = ts * 32 + oc
                    if gx >= 4:
                        tsp, ocp = divmod(gx - 4, 32)
                        s.wait_ge(s_zev, tsp * 64 + ocp + 1)
                    s.dma_start(out=x0s[:, (gx % 4) * 512:(gx % 4 + 1) * 512],
                                in_=x0Th[oc * 128:(oc + 1) * 128,
                                         ts * 512:(ts + 1) * 512]
                                ).then_inc(s_x0[gx % 4], 16)

        @block.tensor
        def _(tns):
            tns.wait_ge(s_cst, 16 * 10)
            for ts in range(NTS):
                base_mm = ts * 64
                tns.wait_ge(s_at, 16 * 32 * (ts + 1))
                for oc in range(32):
                    g2 = ts * 32 + oc
                    gm = ts * 64 + oc
                    tns.wait_ge(s_mw[g2 % 2], 16 * (g2 // 2 + 1))
                    if gm >= 2:
                        tns.wait_ge(s_zev, gm - 1)
                    if oc < 2 and ts >= 1:
                        tns.wait_ge(s_ke, ts * 2)
                    for dc in range(32):
                        ins = tns.matmul(mps[:, (gm % 2) * 512:(gm % 2 + 1) * 512],
                                         wos[:, (g2 % 2) * 4096 + dc * 128:
                                             (g2 % 2) * 4096 + (dc + 1) * 128],
                                         atn[:, dc * 512:(dc + 1) * 512],
                                         start=(dc == 0), stop=(dc == 31))
                    ins.then_inc(s_mm, 1)
                    tns.wait_ge(s_sq, ts * 64 + oc + 1)
                    tns.matmul(st1p[:, :], onbs[:, :],
                               zbf[:, oc * 512:(oc + 1) * 512],
                               start=(oc == 0), stop=(oc == 31))
                    ins = tns.matmul(st2p[:, :], onbs[:, :],
                                     z2s[:, (oc % 2) * 512:(oc % 2 + 1) * 512],
                                     start=(oc == 0), stop=(oc == 31))
                    ins.then_inc(s_st, 1)
                ln = ts * 2
                tns.wait_ge(s_l1, ln + 1)
                tns.matmul(bps[:, :], onfs[:, :], nmean[:, :],
                           start=True, stop=True).then_inc(s_bc, 1)
                tns.wait_ge(s_bv, 2 * ln + 1)
                tns.wait_ge(s_l2, ln + 1)
                tns.matmul(bps[:, :], onfs[:, :], rstd[:, :],
                           start=True, stop=True).then_inc(s_bc, 1)
                for tt in range(4):
                    tns.wait_ge(s_nr, ts * 64 + 32)
                    if ts * 4 + tt >= 1:
                        tns.wait_ge(s_fw, ts * 4 + tt)
                    for dc in range(32):
                        ins = tns.matmul(lps[:, :],
                                         zbf[:, dc * 512 + tt * 128:dc * 512 + (tt + 1) * 128],
                                         rtsb[:, dc * 8:(dc + 1) * 8],
                                         start=(dc == 0), stop=(dc == 31))
                    ins.then_inc(s_fl, 1)
                    for dc in range(32):
                        ins = tns.matmul(hps[:, :],
                                         zbf[:, dc * 512 + tt * 128:dc * 512 + (tt + 1) * 128],
                                         asb[:, dc * 128:(dc + 1) * 128],
                                         start=(dc == 0), stop=(dc == 31))
                    ins.then_inc(s_fh, 1)
                    tns.wait_ge(s_fw, ts * 4 + tt + 1)
                    if ts * 4 + tt >= 1:
                        tns.wait_ge(s_fc, ts * 4 + tt)
                    if ts >= 1 and tt == 0:
                        tns.wait_ge(s_vc, ts * 4)
                    tns.transpose(tp0[:, :], whbf[:, :], idsb[:, :]).then_inc(s_ft, 1)
                tns.wait_ge(s_fc, ts * 4 + 4)
                for oc in range(32):
                    g3 = base_mm + 32 + oc
                    if True:
                        tns.wait_ge(s_zev, g3 - 1)
                    ins = tns.matmul(mps[:, (g3 % 2) * 512:(g3 % 2 + 1) * 512],
                                     bwsb[:, oc * 128:(oc + 1) * 128],
                                     whT[:, :], start=True, stop=True)
                    ins.then_inc(s_mm, 1)
                    tns.wait_ge(s_sq, ts * 64 + 32 + oc + 1)
                    tns.matmul(st1p[:, :], onbs[:, :],
                               zbf[:, oc * 512:(oc + 1) * 512],
                               start=(oc == 0), stop=(oc == 31))
                    ins = tns.matmul(st2p[:, :], onbs[:, :],
                                     z2s[:, (oc % 2) * 512:(oc % 2 + 1) * 512],
                                     start=(oc == 0), stop=(oc == 31))
                    ins.then_inc(s_st, 1)
                ln = ts * 2 + 1
                tns.wait_ge(s_l1, ln + 1)
                tns.matmul(bps[:, :], onfs[:, :], nmean[:, :],
                           start=True, stop=True).then_inc(s_bc, 1)
                tns.wait_ge(s_bv, 2 * ln + 1)
                tns.wait_ge(s_l2, ln + 1)
                tns.matmul(bps[:, :], onfs[:, :], rstd[:, :],
                           start=True, stop=True).then_inc(s_bc, 1)
                tns.wait_ge(s_nr, ts * 64 + 64)
                if ts >= 1:
                    tns.wait_ge(s_ke, ts * 2)
                for dc in range(32):
                    ins = tns.matmul(mps[:, 0:512], wksb[:, dc * 128:(dc + 1) * 128],
                                     zbf[:, dc * 512:(dc + 1) * 512],
                                     start=(dc == 0), stop=(dc == 31))
                ins.then_inc(s_kv, 1)
                for dc in range(32):
                    ins = tns.matmul(mps[:, 512:1024], wvsb[:, dc * 128:(dc + 1) * 128],
                                     zbf[:, dc * 512:(dc + 1) * 512],
                                     start=(dc == 0), stop=(dc == 31))
                ins.then_inc(s_kv, 1)
                for tt in range(4):
                    tns.wait_ge(s_ke, ts * 2 + 2)
                    if ts * 4 + tt >= 1:
                        tns.wait_ge(s_vc, ts * 4 + tt)
                    ins = tns.transpose(tp0[:, :],
                                        kvs[:, 512 + tt * 128:512 + (tt + 1) * 128],
                                        idsb[:, :])
                    ins.then_inc(s_vt, 1)

        @block.vector
        def _(v):
            ss = SelfSync(v, s_vs)
            v.memset(epsb[:, :], 1e-5)
            for ts in range(NTS):
                for phase in range(2):
                    if phase == 0 and ts >= 1:
                        v.wait_ge(s_x2o, 16 * 64 * ts)
                    if phase == 1:
                        v.wait_ge(s_x1o, 16 * 32 * (ts + 1))
                    for oc in range(32):
                        gz = ts * 64 + phase * 32 + oc
                        v.wait_ge(s_mm, gz + 1)
                        if phase == 0:
                            v.wait_ge(s_x0[(ts * 32 + oc) % 4],
                                      16 * ((ts * 32 + oc) // 4 + 1))
                            ss.step(v.tensor_tensor(
                                zf[:, oc * 512:(oc + 1) * 512],
                                mps[:, (gz % 2) * 512:(gz % 2 + 1) * 512],
                                x0s[:, ((ts * 32 + oc) % 4) * 512:
                                    ((ts * 32 + oc) % 4 + 1) * 512],
                                ALU.add))
                            ss.barrier()
                        else:
                            ins = v.tensor_tensor(zf[:, oc * 512:(oc + 1) * 512],
                                                  mps[:, (gz % 2) * 512:(gz % 2 + 1) * 512],
                                                  zf[:, oc * 512:(oc + 1) * 512],
                                                  ALU.add)
                        if phase == 1:
                            ss.step(ins)
                            ss.barrier()
                        v.tensor_copy(zbf[:, oc * 512:(oc + 1) * 512],
                                      zf[:, oc * 512:(oc + 1) * 512]).then_inc(s_zev, 1)
                    ln = ts * 2 + phase
                    v.wait_ge(s_st, ts * 64 + phase * 32 + 32)
                    ss.step(v.tensor_scalar_mul(mean[:, :], st1p[:, :], 1.0 / 4096.0))
                    ss.step(v.tensor_scalar_mul(varv[:, :], st2p[:, :], 1.0 / 4096.0))
                    ss.barrier()
                    ss.step(v.tensor_tensor(nmean[:, :], mean[:, :], mean[:, :],
                                            ALU.mult))
                    ss.barrier()
                    ss.step(v.tensor_tensor(varv[:, :], varv[:, :], nmean[:, :],
                                            ALU.subtract))
                    ss.barrier()
                    v.tensor_scalar_mul(nmean[:, :], mean[:, :], -1.0).then_inc(s_l1, 1)
                    v.wait_ge(s_bc, 2 * ln + 1)
                    v.tensor_copy(nmb[:, :], bps[:, :]).then_inc(s_bv, 1)
                    v.wait_ge(s_bc, 2 * ln + 2)
                    ss.step(v.tensor_copy(rsb[:, :], bps[:, :]))
                    ss.barrier()
                    v.sem_inc(s_bv, 1)
                    for oc in range(32):
                        ss.step(v.tensor_tensor(zf[:, oc * 512:(oc + 1) * 512],
                                                zf[:, oc * 512:(oc + 1) * 512],
                                                nmb[:, :], ALU.add))
                        ss.barrier()
                        ss.step(v.tensor_tensor(zf[:, oc * 512:(oc + 1) * 512],
                                                zf[:, oc * 512:(oc + 1) * 512],
                                                rsb[:, :], ALU.mult))
                        ss.barrier()
                        v.tensor_copy(zbf[:, oc * 512:(oc + 1) * 512],
                                      zf[:, oc * 512:(oc + 1) * 512]).then_inc(s_nr, 1)
                    if phase == 0:
                        for tt in range(4):
                            v.wait_ge(s_fl, ts * 4 + tt + 1)
                            ss.step(v.tensor_copy(lgs[:, :], lps[:, :]))
                            ss.barrier()
                            ss.step(v.reduce_max(mx[:, :], lgs[:, :], axis=AX.X))
                            ss.barrier()
                            ss.step(v.tensor_single_scalar(ge01[:, :], lgs[:, :],
                                                           mx[:, 0:1], ALU.is_ge))
                            ss.barrier()
                            ss.step(v.tensor_tensor(rw[:, :], ge01[:, :], lgs[:, :],
                                                    ALU.mult))
                            ss.barrier()
                            v.sem_inc(s_fr, 1)
                            v.wait_ge(s_fh, ts * 4 + tt + 1)
                            for e in range(8):
                                ins = v.tensor_scalar_mul(
                                    whbf[:, e * 16:(e + 1) * 16],
                                    hps[:, e * 16:(e + 1) * 16], rw[:, e:e + 1])
                            ins.then_inc(s_fw, 1)
                            v.wait_ge(s_ft, ts * 4 + tt + 1)
                            v.tensor_copy(whT[:, tt * 128:(tt + 1) * 128],
                                          tp0[:, :]).then_inc(s_fc, 1)
                v.wait_ge(s_kv, ts * 2 + 1)
                if ts >= 1:
                    v.wait_ge(s_ko, 16 * 5 * ts)
                v.tensor_scalar_add(kvs[:, 0:512], mps[:, 0:512],
                                    kbsb[:, 0:1]).then_inc(s_ke, 1)
                v.wait_ge(s_kv, ts * 2 + 2)
                v.tensor_scalar_add(kvs[:, 512:1024], mps[:, 512:1024],
                                    vbsb[:, 0:1]).then_inc(s_ke, 1)
                for tt in range(4):
                    v.wait_ge(s_vt, ts * 4 + tt + 1)
                    v.tensor_copy(vts[:, tt * 128:(tt + 1) * 128],
                                  tp0[:, :]).then_inc(s_vc, 1)

        @block.scalar
        def _(sc):
            for ts in range(NTS):
                for phase in range(2):
                    for oc in range(32):
                        gz = ts * 64 + phase * 32 + oc
                        sc.wait_ge(s_zev, gz + 1)
                        if gz >= 2:
                            sc.wait_ge(s_st, gz - 1)
                        sc.activation(z2s[:, (oc % 2) * 512:(oc % 2 + 1) * 512],
                                      zbf[:, oc * 512:(oc + 1) * 512],
                                      AF.Square).then_inc(s_sq, 1)
                    ln = ts * 2 + phase
                    sc.wait_ge(s_l1, ln + 1)
                    sc.activation(rstd[:, :], varv[:, :], AF.Ln,
                                  bias=epsb[0:1, 0:1]).then_inc(s_as, 1)
                    sc.wait_ge(s_as, ln + 1)
                    sc.activation(rstd[:, :], rstd[:, :], AF.Exp,
                                  scale=-0.5).then_inc(s_l2, 1)
                    for oc in range(32):
                        sc.wait_ge(s_nr, ts * 64 + phase * 32 + oc + 1)
                        if phase == 0:
                            sc.dma_start(out=x1T_bf[oc * 128:(oc + 1) * 128,
                                                    ts * 512:(ts + 1) * 512],
                                         in_=zbf[:, oc * 512:(oc + 1) * 512]
                                         ).then_inc(s_x1o, 16)
                        else:
                            sc.dma_start(out=x2T_f32[oc * 128:(oc + 1) * 128,
                                                     ts * 512:(ts + 1) * 512],
                                         in_=zf[:, oc * 512:(oc + 1) * 512]
                                         ).then_inc(s_x2o, 16)
                            sc.dma_start(out=x2T_bf[oc * 128:(oc + 1) * 128,
                                                    ts * 512:(ts + 1) * 512],
                                         in_=zbf[:, oc * 512:(oc + 1) * 512]
                                         ).then_inc(s_x2o, 16)
                sc.wait_ge(s_ke, ts * 2 + 2)
                sc.dma_start(out=k1T_h[:, ts * 512:(ts + 1) * 512],
                             in_=kvs[:, 0:512]).then_inc(s_ko, 16)
                for tt in range(4):
                    sc.wait_ge(s_vc, ts * 4 + tt + 1)
                    sc.dma_start(out=v1tok_h[ts * 512 + tt * 128:
                                             ts * 512 + (tt + 1) * 128, :],
                                 in_=vts[:, tt * 128:(tt + 1) * 128]).then_inc(s_ko, 16)
            sc.wait_ge(s_x1o, 16 * 128)
            sc.wait_ge(s_x2o, 16 * 256)
            sc.wait_ge(s_ko, 16 * 20)

    return nc



def build_phaseD():
    nc = _mknc()
    x2sufT = nc.dram_tensor("x2sufT", [4096, 2048], BF16, kind="ExternalInput").ap()
    wq1_t = nc.dram_tensor("wq1_t", [16, 128, 32, 128], F32, kind="ExternalInput").ap()
    wq1_b = nc.dram_tensor("wq1_b", [128, 16], F32, kind="ExternalInput").ap()
    k1T = nc.dram_tensor("k1T", [128, 4096], BF16, kind="ExternalInput").ap()
    v1tok = nc.dram_tensor("v1tok", [4096, 128], BF16, kind="ExternalInput").ap()
    maskp = nc.dram_tensor("maskp", [4, 128, 512], F32, kind="ExternalInput").ap()
    onesb = nc.dram_tensor("onesb", [128, 1], BF16, kind="ExternalInput").ap()
    onesf = nc.dram_tensor("onesf", [1, 128], F32, kind="ExternalInput").ap()
    attn1T = nc.dram_tensor("attn1T", [2048, 2048], BF16, kind="ExternalOutput").ap()
    q1T = nc.dram_tensor("q1T", [2048, 2048], BF16, kind="Internal").ap()

    NHD = 16

    with ExitStack() as ctx:
        sb = lambda n, sh, dt: ctx.enter_context(nc.sbuf_tensor(n, sh, dt))
        sem = lambda n: ctx.enter_context(nc.semaphore(n))
        x2s = sb("x2s", [128, 32 * 2048], BF16)
        wqs = sb("wqs", [128, 2 * 4096], BF16)
        qbs = sb("qbs", [128, 16], F32)
        evq = sb("evq", [128, 4 * 512], BF16)
        k1s = sb("k1s", [128, 4096], BF16)
        v1s = sb("v1s", [128, 32 * 128], BF16)
        q1s = sb("q1s", [128, 2 * 2048], BF16)
        ptile = sb("ptile", [128, 3 * 512], BF16)
        masks = sb("masks", [128, 4 * 512], F32)
        onbs = sb("onbs", [128, 1], BF16)
        onfs = sb("onfs", [1, 128], F32)
        rl = sb("rl", [1, 512], F32)
        rlb = sb("rlb", [128, 512], F32)
        nshift = sb("nshift", [128, 1], F32)
        attns = sb("attns", [128, 2 * 512], BF16)
        spsum = ctx.enter_context(nc.psum_tensor("spsum", [128, 2 * 512], F32))
        opsum = ctx.enter_context(nc.psum_tensor("opsum", [128, 512], F32))
        lpsum = ctx.enter_context(nc.psum_tensor("lpsum", [1, 512], F32))
        bpsum = ctx.enter_context(nc.psum_tensor("bpsum", [128, 512], F32))
        s_pre = sem("s_pre")
        s_ns = sem("s_ns")
        s_x = sem("s_x")
        s_w = [sem("s_w0"), sem("s_w1")]
        s_mm = sem("s_mm")
        s_ev = sem("s_ev")
        s_qo = sem("s_qo")
        s_ld = [sem("s_ld0"), sem("s_ld1")]
        s_s = sem("s_s")
        s_mk = sem("s_mk")
        s_p = sem("s_p")
        s_pc = sem("s_pc")
        s_rcp = sem("s_rcp")
        s_bcm = sem("s_bcm")
        s_oev = sem("s_oev")
        s_ao = [sem("s_ao0"), sem("s_ao1")]
        s_vs = sem("s_vs")
        block = ctx.enter_context(nc.Block())

        NQG = 4
        CPH = 72

        def grp_info(blocks):
            out = []
            for sb_i, b in enumerate(blocks):
                for gl in range(2):
                    q0_local = sb_i * 1024 + gl * 512
                    nk = b * 8 + gl * 4 + 4
                    out.append((q0_local, nk))
            return out

        def cum_chunks(blocks, gi):
            info = grp_info(blocks)
            return sum(nk for _, nk in info[:gi])

        def pt_reg(eng):
            pid = eng.partition_id()
            r = eng.alloc_register("ptype")
            eng.reg_alu(r, pid, 1, ALU.arith_shift_right)
            eng.reg_alu(r, r, 1, ALU.bitwise_and)
            return r

        @block.sync
        def _(s):
            for rel in range(4):
                s.dma_start(out=masks[:, rel * 512:(rel + 1) * 512],
                            in_=maskp[rel]).then_inc(s_pre, 16)
            s.dma_start(out=onbs[:], in_=onesb[:, :]).then_inc(s_pre, 16)
            s.dma_start(out=onfs[:], in_=onesf[:, :]).then_inc(s_pre, 16)
            s.dma_start(out=k1s[:], in_=k1T[:, :]).then_inc(s_pre, 16)
            for kc in range(32):
                s.dma_start(out=v1s[:, kc * 128:(kc + 1) * 128],
                            in_=v1tok[kc * 128:(kc + 1) * 128, :]).then_inc(s_pre, 16)
            s.dma_start(out=qbs[:], in_=wq1_b[:, :]).then_inc(s_pre, 16)
            for dc in range(32):
                s.dma_start(out=x2s[:, dc * 2048:(dc + 1) * 2048],
                            in_=x2sufT[dc * 128:(dc + 1) * 128, :]).then_inc(s_x, 16)
            for h in range(NHD):
                if h < 2:
                    s.wait_ge(s_qo, 16 * 64)
                else:
                    s.wait_ge(s_pc, CPH * (h - 1))
                s.dma_start(out=q1s[:, (h % 2) * 2048:(h % 2 + 1) * 2048],
                            in_=q1T[h * 128:(h + 1) * 128, :]
                            ).then_inc(s_ld[h % 2], 16)

        @block.gpsimd
        def _(g):
            for cb in range(16):
                if cb >= 2:
                    g.wait_ge(s_ev, (cb - 1) * 4)
                g.dma_start(out=wqs[:, (cb % 2) * 4096:(cb % 2 + 1) * 4096],
                            in_=wq1_t[cb].rearrange("p a b -> p (a b)")
                            ).then_inc(s_w[cb % 2], 16)

        @block.tensor
        def _(tns):
            tns.wait_ge(s_x, 16 * 32)
            for cb in range(16):
                tns.wait_ge(s_w[cb % 2], 16 * (cb // 2 + 1))
                for tb in range(4):
                    gq = cb * 4 + tb
                    if gq >= 2:
                        tns.wait_ge(s_ev, gq - 1)
                    for dc in range(32):
                        ins = tns.matmul(
                            spsum[:, (gq % 2) * 512:(gq % 2 + 1) * 512],
                            wqs[:, (cb % 2) * 4096 + dc * 128:
                                (cb % 2) * 4096 + (dc + 1) * 128],
                            x2s[:, dc * 2048 + tb * 512:dc * 2048 + (tb + 1) * 512],
                            start=(dc == 0), stop=(dc == 31))
                    ins.then_inc(s_mm, 1)
            tns.wait_ge(s_pre, 16 * 40)
            rp = pt_reg(tns)
            for var, blocks in ((0, (0, 3)), (1, (1, 2))):
                guard = tns.If_eq(rp, 0) if var == 0 else tns.Else()
                with guard:
                    info = grp_info(blocks)
                    for h in range(NHD):
                        tns.wait_ge(s_ld[h % 2], 16 * (h // 2 + 1))
                        for gi, (q0, nk) in enumerate(info):
                            j = h * NQG + gi
                            base = h * CPH + cum_chunks(blocks, gi)

                            def sum_pv(kc, base=base, q0=q0, nk=nk, j=j, h=h):
                                i = base + kc
                                tns.wait_ge(s_p, i + 1)
                                if kc == 0:
                                    tns.wait_ge(s_oev, j)
                                pt = ptile[:, (i % 3) * 512:(i % 3 + 1) * 512]
                                tns.matmul(lpsum[:, :], onbs[:, :], pt,
                                           start=(kc == 0), stop=(kc == nk - 1))
                                ins = tns.matmul(
                                    opsum[:, :],
                                    v1s[:, kc * 128:(kc + 1) * 128],
                                    pt, start=(kc == 0), stop=(kc == nk - 1))
                                ins.then_inc(s_pc, 1)

                            for kc in range(nk):
                                i = base + kc
                                if i >= 2:
                                    tns.wait_ge(s_p, i - 1)
                                ins = tns.matmul(
                                    spsum[:, (i % 2) * 512:(i % 2 + 1) * 512],
                                    k1s[:, kc * 128:(kc + 1) * 128],
                                    q1s[:, (h % 2) * 2048 + q0:
                                        (h % 2) * 2048 + q0 + 512],
                                    start=True, stop=True)
                                ins.then_inc(s_s, 1)
                                if kc >= 1:
                                    sum_pv(kc - 1)
                            sum_pv(nk - 1)
                            tns.wait_ge(s_rcp, j + 1)
                            tns.matmul(bpsum[:, :], onfs[:, :], rl[:, :],
                                       start=True, stop=True).then_inc(s_bcm, 1)

        @block.vector
        def _(v):
            ss = SelfSync(v, s_vs)
            v.memset(nshift[:, :], -EXP_SHIFT).then_inc(s_ns, 1)
            for cb in range(16):
                for tb in range(4):
                    gq = cb * 4 + tb
                    v.wait_ge(s_mm, gq + 1)
                    if gq >= 4:
                        v.wait_ge(s_qo, 16 * (gq - 3))
                    v.tensor_scalar_add(evq[:, (gq % 4) * 512:(gq % 4 + 1) * 512],
                                        spsum[:, (gq % 2) * 512:(gq % 2 + 1) * 512],
                                        qbs[:, cb:cb + 1]).then_inc(s_ev, 1)
            rp = pt_reg(v)
            n0 = ss.n
            for var, blocks in ((0, (0, 3)), (1, (1, 2))):
                ss.n = n0
                guard = v.If_eq(rp, 0) if var == 0 else v.Else()
                with guard:
                    info = grp_info(blocks)
                    for h in range(NHD):
                        for gi, (q0, nk) in enumerate(info):
                            j = h * NQG + gi
                            base = h * CPH + cum_chunks(blocks, gi)
                            for kc in range(nk - 4, nk):
                                i = base + kc
                                rel = kc - (nk - 4)
                                v.wait_ge(s_s, i + 1)
                                v.tensor_tensor(
                                    spsum[:, (i % 2) * 512:(i % 2 + 1) * 512],
                                    spsum[:, (i % 2) * 512:(i % 2 + 1) * 512],
                                    masks[:, rel * 512:(rel + 1) * 512],
                                    ALU.add).then_inc(s_mk, 1)
                            v.wait_ge(s_pc, base + nk)
                            v.reciprocal(rl[:, :], lpsum[:, :]).then_inc(s_rcp, 1)
                            v.wait_ge(s_bcm, j + 1)
                            ss.step(v.tensor_copy(rlb[:, :], bpsum[:, :]))
                            ss.barrier()
                            if j >= 2:
                                v.wait_ge(s_ao[j % 2], 16 * (j // 2))
                            v.scalar_tensor_tensor(
                                attns[:, (j % 2) * 512:(j % 2 + 1) * 512],
                                opsum[:, :], 1.0, rlb[:, :],
                                ALU.mult, ALU.mult).then_inc(s_oev, 1)

        @block.scalar
        def _(sc):
            for cb in range(16):
                for tb in range(4):
                    gq = cb * 4 + tb
                    sc.wait_ge(s_ev, gq + 1)
                    sc.dma_start(out=q1T[cb * 128:(cb + 1) * 128,
                                         tb * 512:(tb + 1) * 512],
                                 in_=evq[:, (gq % 4) * 512:(gq % 4 + 1) * 512]
                                 ).then_inc(s_qo, 16)
            sc.wait_ge(s_ns, 1)
            rp = pt_reg(sc)
            for var, blocks in ((0, (0, 3)), (1, (1, 2))):
                guard = sc.If_eq(rp, 0) if var == 0 else sc.Else()
                with guard:
                    info = grp_info(blocks)
                    for h in range(NHD):
                        for gi, (q0, nk) in enumerate(info):
                            j = h * NQG + gi
                            base = h * CPH + cum_chunks(blocks, gi)
                            dbase = (h * NQG + gi) * 4 - 4 * (h * NQG + gi) \
                                + 4 * (h * NQG + gi)
                            for kc in range(nk):
                                i = base + kc
                                if i >= 3:
                                    sc.wait_ge(s_pc, i - 2)
                                if kc >= nk - 4:
                                    sc.wait_ge(s_mk, 4 * j + (kc - (nk - 4)) + 1)
                                else:
                                    sc.wait_ge(s_s, i + 1)
                                sc.activation(ptile[:, (i % 3) * 512:(i % 3 + 1) * 512],
                                              spsum[:, (i % 2) * 512:(i % 2 + 1) * 512],
                                              AF.Exp, bias=nshift[:, 0:1],
                                              scale=SC1).then_inc(s_p, 1)
                            sc.wait_ge(s_oev, j + 1)
                            sc.dma_start(
                                out=attn1T[h * 128:(h + 1) * 128, q0:q0 + 512],
                                in_=attns[:, (j % 2) * 512:(j % 2 + 1) * 512]
                            ).then_inc(s_ao[j % 2], 16)
            sc.wait_ge(s_ao[0], 16 * NHD * 2)
            sc.wait_ge(s_ao[1], 16 * NHD * 2)

    return nc



def build_phaseE():
    nc = _mknc(scratch=8192)
    attn1_my = nc.dram_tensor("attn1_my", [4096, 1024], BF16, kind="ExternalInput").ap()
    x2myT = nc.dram_tensor("x2myT", [4096, 1024], F32, kind="ExternalInput").ap()
    wo1_t = nc.dram_tensor("wo1_t", [32, 128, 32, 128], F32, kind="ExternalInput").ap()
    wo1_b = nc.dram_tensor("wo1_b", [128, 32], F32, kind="ExternalInput").ap()
    fwcb = nc.dram_tensor("fwcb", [32, 128, 32, 128], F16, kind="ExternalInput").ap()
    fscb = nc.dram_tensor("fscb", [32, 128, 32, 128], F16, kind="ExternalInput").ap()
    onesb = nc.dram_tensor("onesb", [128, 1], BF16, kind="ExternalInput").ap()
    onesf = nc.dram_tensor("onesf", [1, 128], F32, kind="ExternalInput").ap()
    outT = nc.dram_tensor("outT", [4096, 1024], F32, kind="ExternalOutput").ap()
    x3dbg = nc.dram_tensor("x3dbg", [4096, 1024], BF16, kind="ExternalOutput").ap()
    y1dbg = nc.dram_tensor("y1dbg", [4096, 1024], F16, kind="ExternalOutput").ap()
    wo1_bf = nc.dram_tensor("wo1_bf", [32, 128, 4096], BF16, kind="Internal").ap()

    NTS = 2

    with ExitStack() as ctx:
        sb = lambda n, sh, dt: ctx.enter_context(nc.sbuf_tensor(n, sh, dt))
        sem = lambda n: ctx.enter_context(nc.semaphore(n))
        atn = sb("atn", [128, 32 * 512], BF16)
        zf = sb("zf", [128, 32 * 512], F32)
        zbf = sb("zbf", [128, 32 * 512], BF16)
        wos = sb("wos", [128, 2 * 4096], BF16)
        x2s = sb("x2s", [128, 2 * 512], F32)
        fws = sb("fws", [128, 2 * 4096], F16)
        fss = sb("fss", [128, 2 * 4096], F16)
        w1s = sb("w1s", [128, 2 * 4096], BF16)
        wbs = sb("wbs", [128, 32], F32)
        onbs = sb("onbs", [128, 1], BF16)
        onfs = sb("onfs", [1, 128], F32)
        y16 = sb("y16", [128, 512], F16)
        y16d = sb("y16d", [128, 2 * 512], F16)
        z2s = sb("z2s", [128, 2 * 512], BF16)
        varv = sb("varv", [1, 512], F32)
        nmean = sb("nmean", [1, 512], F32)
        rstd = sb("rstd", [1, 512], F32)
        epsb = sb("epsb", [1, 1], F32)
        nmb = sb("nmb", [128, 512], F32)
        rsb = sb("rsb", [128, 512], F32)
        mps = ctx.enter_context(nc.psum_tensor("mps", [128, 2 * 512], F32))
        st1p = ctx.enter_context(nc.psum_tensor("st1p", [1, 512], F32))
        st2p = ctx.enter_context(nc.psum_tensor("st2p", [1, 512], F32))
        bps = ctx.enter_context(nc.psum_tensor("bps", [128, 512], F32))
        s_cst = sem("s_cst")
        s_wc = sem("s_wc")
        s_at = sem("s_at")
        s_x2 = [sem(f"s_x2{i}") for i in range(2)]
        s_mw = [sem("s_mw0"), sem("s_mw1")]
        s_fw = [sem("s_fw0"), sem("s_fw1")]
        s_dq = sem("s_dq")
        s_mm = sem("s_mm")
        s_zev = sem("s_zev")
        s_sq = sem("s_sq")
        s_st = sem("s_st")
        s_l1 = sem("s_l1")
        s_l2 = sem("s_l2")
        s_bc = sem("s_bc")
        s_bv = sem("s_bv")
        s_nr = sem("s_nr")
        s_oo = sem("s_oo")
        s_vs = sem("s_vs")
        s_as = sem("s_as")
        s_dbg = sem("s_dbg")
        s_y16 = sem("s_y16")
        s_ydma = sem("s_ydma")
        s_zb = sem("s_zb")
        block = ctx.enter_context(nc.Block())

        @block.gpsimd
        def _(g):
            g.dma_start(out=wbs[:], in_=wo1_b[:, :]).then_inc(s_cst, 16)
            g.dma_start(out=onbs[:], in_=onesb[:, :]).then_inc(s_cst, 16)
            g.dma_start(out=onfs[:], in_=onesf[:, :]).then_inc(s_cst, 16)
            for cb in range(32):
                g.dma_start(out=wo1_bf[cb],
                            in_=wo1_t[cb].rearrange("p a b -> p (a b)")
                            ).then_inc(s_wc, 16)
            for ts in range(NTS):
                for oc in range(32):
                    gq = ts * 32 + oc
                    g.wait_ge(s_fw[gq % 2], 16 * 2 * (gq // 2 + 1))
                    if gq >= 2:
                        tsp, ocp = divmod(gq - 2, 32)
                        g.wait_ge(s_mm, tsp * 64 + 32 + ocp + 1)
                    g.tensor_tensor(w1s[:, (gq % 2) * 4096:(gq % 2 + 1) * 4096],
                                    fws[:, (gq % 2) * 4096:(gq % 2 + 1) * 4096],
                                    fss[:, (gq % 2) * 4096:(gq % 2 + 1) * 4096],
                                    ALU.mult).then_inc(s_dq, 1)
                    gy = gq - 1
                    if gy >= 0:
                        g.wait_ge(s_y16, gy + 1)
                        g.dma_start(out=y1dbg[(gy % 32) * 128:(gy % 32 + 1) * 128,
                                              (gy // 32) * 512:(gy // 32 + 1) * 512],
                                    in_=y16d[:, (gy % 2) * 512:(gy % 2 + 1) * 512]
                                    ).then_inc(s_ydma, 16)
            g.wait_ge(s_y16, 64)
            gy = 63
            g.dma_start(out=y1dbg[(gy % 32) * 128:(gy % 32 + 1) * 128,
                                  (gy // 32) * 512:(gy // 32 + 1) * 512],
                        in_=y16d[:, (gy % 2) * 512:(gy % 2 + 1) * 512]
                        ).then_inc(s_ydma, 16)

        @block.sync
        def _(s):
            for ts in range(NTS):
                if ts >= 1:
                    s.wait_ge(s_mm, ts * 64)
                for dc in range(32):
                    s.dma_start(out=atn[:, dc * 512:(dc + 1) * 512],
                                in_=attn1_my[dc * 128:(dc + 1) * 128,
                                             ts * 512:(ts + 1) * 512]).then_inc(s_at, 16)
                for oc in range(32):
                    g2 = ts * 32 + oc
                    s.wait_ge(s_wc, 16 * (oc + 1))
                    if g2 >= 2:
                        tsp, ocp = divmod(g2 - 2, 32)
                        s.wait_ge(s_mm, tsp * 64 + ocp + 1)
                    s.dma_start(out=wos[:, (g2 % 2) * 4096:(g2 % 2 + 1) * 4096],
                                in_=wo1_bf[oc]).then_inc(s_mw[g2 % 2], 16)
                    gx = ts * 32 + oc
                    if gx >= 2:
                        tsp, ocp = divmod(gx - 2, 32)
                        s.wait_ge(s_zev, tsp * 64 + ocp + 1)
                    s.dma_start(out=x2s[:, (gx % 2) * 512:(gx % 2 + 1) * 512],
                                in_=x2myT[oc * 128:(oc + 1) * 128,
                                          ts * 512:(ts + 1) * 512]
                                ).then_inc(s_x2[gx % 2], 16)
                for oc in range(32):
                    gq = ts * 32 + oc
                    if gq >= 2:
                        s.wait_ge(s_dq, gq - 1)
                    s.dma_start(out=fws[:, (gq % 2) * 4096:(gq % 2 + 1) * 4096],
                                in_=fwcb[oc].rearrange("p a b -> p (a b)")
                                ).then_inc(s_fw[gq % 2], 16)
                    s.dma_start(out=fss[:, (gq % 2) * 4096:(gq % 2 + 1) * 4096],
                                in_=fscb[oc].rearrange("p a b -> p (a b)")
                                ).then_inc(s_fw[gq % 2], 16)

        @block.tensor
        def _(tns):
            tns.wait_ge(s_cst, 48)
            for ts in range(NTS):
                tns.wait_ge(s_at, 16 * 32 * (ts + 1))
                for oc in range(32):
                    g2 = ts * 32 + oc
                    gm = ts * 64 + oc
                    tns.wait_ge(s_mw[g2 % 2], 16 * (g2 // 2 + 1))
                    if gm >= 2:
                        tns.wait_ge(s_zev, gm - 1)
                    for dc in range(32):
                        ins = tns.matmul(mps[:, (gm % 2) * 512:(gm % 2 + 1) * 512],
                                         wos[:, (g2 % 2) * 4096 + dc * 128:
                                             (g2 % 2) * 4096 + (dc + 1) * 128],
                                         atn[:, dc * 512:(dc + 1) * 512],
                                         start=(dc == 0), stop=(dc == 31))
                    ins.then_inc(s_mm, 1)
                    tns.wait_ge(s_sq, ts * 64 + oc + 1)
                    tns.matmul(st1p[:, :], onbs[:, :],
                               zbf[:, oc * 512:(oc + 1) * 512],
                               start=(oc == 0), stop=(oc == 31))
                    ins = tns.matmul(st2p[:, :], onbs[:, :],
                                     z2s[:, (oc % 2) * 512:(oc % 2 + 1) * 512],
                                     start=(oc == 0), stop=(oc == 31))
                    ins.then_inc(s_st, 1)
                ln = ts * 2
                tns.wait_ge(s_l1, ln + 1)
                tns.matmul(bps[:, :], onfs[:, :], nmean[:, :],
                           start=True, stop=True).then_inc(s_bc, 1)
                tns.wait_ge(s_bv, 2 * ln + 1)
                tns.wait_ge(s_l2, ln + 1)
                tns.matmul(bps[:, :], onfs[:, :], rstd[:, :],
                           start=True, stop=True).then_inc(s_bc, 1)
                tns.wait_ge(s_nr, ts * 64 + 32)
                for oc in range(32):
                    gm = ts * 64 + 32 + oc
                    gq = ts * 32 + oc
                    tns.wait_ge(s_dq, gq + 1)
                    if gm >= 2:
                        tns.wait_ge(s_zev, gm - 1)
                    for ic in range(32):
                        ins = tns.matmul(mps[:, (gm % 2) * 512:(gm % 2 + 1) * 512],
                                         w1s[:, (gq % 2) * 4096 + ic * 128:
                                             (gq % 2) * 4096 + (ic + 1) * 128],
                                         zbf[:, ic * 512:(ic + 1) * 512],
                                         start=(ic == 0), stop=(ic == 31))
                    ins.then_inc(s_mm, 1)
                for oc in range(32):
                    tns.wait_ge(s_sq, ts * 64 + 32 + oc + 1)
                    tns.matmul(st1p[:, :], onbs[:, :],
                               zbf[:, oc * 512:(oc + 1) * 512],
                               start=(oc == 0), stop=(oc == 31))
                    ins = tns.matmul(st2p[:, :], onbs[:, :],
                                     z2s[:, (oc % 2) * 512:(oc % 2 + 1) * 512],
                                     start=(oc == 0), stop=(oc == 31))
                    ins.then_inc(s_st, 1)
                ln = ts * 2 + 1
                tns.wait_ge(s_l1, ln + 1)
                tns.matmul(bps[:, :], onfs[:, :], nmean[:, :],
                           start=True, stop=True).then_inc(s_bc, 1)
                tns.wait_ge(s_bv, 2 * ln + 1)
                tns.wait_ge(s_l2, ln + 1)
                tns.matmul(bps[:, :], onfs[:, :], rstd[:, :],
                           start=True, stop=True).then_inc(s_bc, 1)

        @block.vector
        def _(v):
            ss = SelfSync(v, s_vs)
            v.memset(epsb[:, :], 1e-5)
            for ts in range(NTS):
                for phase in range(2):
                    if phase == 0 and ts >= 1:
                        v.wait_ge(s_oo, 16 * 32 * ts)
                    for oc in range(32):
                        gz = ts * 64 + phase * 32 + oc
                        v.wait_ge(s_mm, gz + 1)
                        if phase == 0:
                            gx = ts * 32 + oc
                            v.wait_ge(s_x2[gx % 2], 16 * (gx // 2 + 1))
                            ss.step(v.scalar_tensor_tensor(
                                zf[:, oc * 512:(oc + 1) * 512],
                                mps[:, (gz % 2) * 512:(gz % 2 + 1) * 512],
                                wbs[:, oc:oc + 1],
                                x2s[:, (gx % 2) * 512:(gx % 2 + 1) * 512],
                                ALU.add, ALU.add))
                            ss.barrier()
                        else:
                            ss.step(v.tensor_copy(
                                y16[:, :],
                                mps[:, (gz % 2) * 512:(gz % 2 + 1) * 512]))
                            ss.barrier()
                            gy = ts * 32 + oc
                            if gy >= 2:
                                v.wait_ge(s_ydma, 16 * (gy - 1))
                            v.tensor_copy(y16d[:, (gy % 2) * 512:(gy % 2 + 1) * 512],
                                          y16[:, :]).then_inc(s_y16, 1)
                            ss.step(v.tensor_tensor(zf[:, oc * 512:(oc + 1) * 512],
                                                    y16[:, :],
                                                    zf[:, oc * 512:(oc + 1) * 512],
                                                    ALU.add))
                            ss.barrier()
                        if phase == 0:
                            v.tensor_copy(zbf[:, oc * 512:(oc + 1) * 512],
                                          zf[:, oc * 512:(oc + 1) * 512]
                                          ).then_inc(s_zev, 1)
                        else:
                            v.sem_inc(s_zev, 1)
                    if phase == 1:
                        v.wait_ge(s_mm, ts * 64 + 64)
                        for oc in range(32):
                            v.tensor_copy(zbf[:, oc * 512:(oc + 1) * 512],
                                          zf[:, oc * 512:(oc + 1) * 512]
                                          ).then_inc(s_zb, 1)
                    ln = ts * 2 + phase
                    v.wait_ge(s_st, ts * 64 + phase * 32 + 32)
                    ss.step(v.tensor_scalar_mul(nmean[:, :], st1p[:, :],
                                                -1.0 / 4096.0))
                    ss.step(v.tensor_scalar_mul(varv[:, :], st2p[:, :], 1.0 / 4096.0))
                    ss.barrier()
                    ss.step(v.tensor_tensor(rstd[:, :], nmean[:, :], nmean[:, :],
                                            ALU.mult))
                    ss.barrier()
                    ss.step(v.tensor_tensor(varv[:, :], varv[:, :], rstd[:, :],
                                            ALU.subtract))
                    ss.barrier()
                    v.sem_inc(s_l1, 1)
                    v.wait_ge(s_bc, 2 * ln + 1)
                    v.tensor_copy(nmb[:, :], bps[:, :]).then_inc(s_bv, 1)
                    v.wait_ge(s_bc, 2 * ln + 2)
                    ss.step(v.tensor_copy(rsb[:, :], bps[:, :]))
                    ss.barrier()
                    v.sem_inc(s_bv, 1)
                    for oc in range(32):
                        ss.step(v.tensor_tensor(zf[:, oc * 512:(oc + 1) * 512],
                                                zf[:, oc * 512:(oc + 1) * 512],
                                                nmb[:, :], ALU.add))
                        ss.barrier()
                        ins = v.tensor_tensor(zf[:, oc * 512:(oc + 1) * 512],
                                              zf[:, oc * 512:(oc + 1) * 512],
                                              rsb[:, :], ALU.mult)
                        if phase == 0:
                            ss.step(ins)
                            ss.barrier()
                            v.tensor_copy(zbf[:, oc * 512:(oc + 1) * 512],
                                          zf[:, oc * 512:(oc + 1) * 512]
                                          ).then_inc(s_nr, 1)
                        else:
                            ins.then_inc(s_nr, 1)

        @block.scalar
        def _(sc):
            for ts in range(NTS):
                for phase in range(2):
                    for oc in range(32):
                        gz = ts * 64 + phase * 32 + oc
                        if phase == 0:
                            sc.wait_ge(s_zev, gz + 1)
                        else:
                            sc.wait_ge(s_zb, ts * 32 + oc + 1)
                        if gz >= 2:
                            sc.wait_ge(s_st, gz - 1)
                        sc.activation(z2s[:, (oc % 2) * 512:(oc % 2 + 1) * 512],
                                      zbf[:, oc * 512:(oc + 1) * 512],
                                      AF.Square).then_inc(s_sq, 1)
                    ln = ts * 2 + phase
                    sc.wait_ge(s_l1, ln + 1)
                    sc.activation(rstd[:, :], varv[:, :], AF.Ln,
                                  bias=epsb[0:1, 0:1]).then_inc(s_as, 1)
                    sc.wait_ge(s_as, ln + 1)
                    sc.activation(rstd[:, :], rstd[:, :], AF.Exp,
                                  scale=-0.5).then_inc(s_l2, 1)
                    if phase == 0:
                        for oc in range(32):
                            sc.wait_ge(s_nr, ts * 64 + oc + 1)
                            sc.dma_start(out=x3dbg[oc * 128:(oc + 1) * 128,
                                                   ts * 512:(ts + 1) * 512],
                                         in_=zbf[:, oc * 512:(oc + 1) * 512]
                                         ).then_inc(s_dbg, 16)
                    if phase == 1:
                        for oc in range(32):
                            sc.wait_ge(s_nr, ts * 64 + 32 + oc + 1)
                            sc.dma_start(out=outT[oc * 128:(oc + 1) * 128,
                                                  ts * 512:(ts + 1) * 512],
                                         in_=zf[:, oc * 512:(oc + 1) * 512]
                                         ).then_inc(s_oo, 16)
            sc.wait_ge(s_oo, 16 * 64)

    return nc



_NC_CACHE = {}


def _nc(phase):
    if phase not in _NC_CACHE:
        _NC_CACHE[phase] = {"A": build_phaseA, "B": build_phaseB, "C": build_phaseC,
                            "D": build_phaseD, "E": build_phaseE}[phase]()
    return _NC_CACHE[phase]


def kernel(**inputs):
    cores = stage_inputs(inputs)
    CIDS = list(range(NC))

    res = run_bass_kernel_spmd(_nc("A"), [
        {"x0T": cores[c]["x0T"], "qkv_t": cores[c]["qkv_t"],
         "cos_fm": cores[c]["cos_fm"], "sin_fm": cores[c]["sin_fm"],
         "P64": cores[c]["P64"]} for c in CIDS], core_ids=CIDS)
    qkvT = [res.results[c]["qkvT"] for c in CIDS]

    res = run_bass_kernel_spmd(_nc("B"), [
        {"qkvT": qkvT[c], "maskp": cores[c]["maskp"], "identt": cores[c]["ident"],
         "onesb": cores[c]["ones_bf"], "onesf": cores[c]["ones_f32"]}
        for c in CIDS], core_ids=CIDS)
    attnT = [res.results[c]["attnT"] for c in CIDS]

    in_maps = []
    for c in CIDS:
        seq, ptype, par, half, blocks, myblk = core_geometry(c)
        pe = (c // 2) * 2
        hsl = slice(*half)
        attn_my = np.concatenate([attnT[pe][:, hsl], attnT[pe + 1][:, hsl]])
        st = cores[c]
        in_maps.append({
            "attn_my": np.ascontiguousarray(attn_my),
            "x0Th": np.ascontiguousarray(st["x0T"][:, hsl]),
            "wo_t": st["wo_t"], "route_rhs": st["route_rhs"], "A_rhs": st["A_rhs"],
            "Bw_lhsT": st["Bw_lhsT"], "wk1_t": st["wk1_t"], "wv1_t": st["wv1_t"],
            "wk1_b": st["wk1_b"], "wv1_b": st["wv1_b"], "identt": st["ident"],
            "onesb": st["ones_bf"], "onesf": st["ones_f32"]})
    res = run_bass_kernel_spmd(_nc("C"), in_maps, core_ids=CIDS)
    x2f = [res.results[c]["x2T_f32"] for c in CIDS]
    x2b = [res.results[c]["x2T_bf"] for c in CIDS]
    k1h = [res.results[c]["k1T_h"] for c in CIDS]
    v1h = [res.results[c]["v1tok_h"] for c in CIDS]

    in_maps = []
    for c in CIDS:
        seq, ptype, par, half, blocks, myblk = core_geometry(c)
        pe = (c // 2) * 2
        st = cores[c]

        def blk_arr(b):
            owner = pe if b * 1024 < HALF * (1) and b < 2 else None
            if b * 1024 < 2048:
                return x2b[pe][:, b * 1024:(b + 1) * 1024]
            return x2b[pe + 1][:, b * 1024 - 2048:(b + 1) * 1024 - 2048]

        x2suf = np.concatenate([blk_arr(blocks[0]), blk_arr(blocks[1])], axis=1)
        k1 = np.concatenate([k1h[pe], k1h[pe + 1]], axis=1)
        v1 = np.concatenate([v1h[pe], v1h[pe + 1]], axis=0)
        in_maps.append({
            "x2sufT": np.ascontiguousarray(x2suf),
            "wq1_t": st["wq1_t"], "wq1_b": st["wq1_b"],
            "k1T": np.ascontiguousarray(k1), "v1tok": np.ascontiguousarray(v1),
            "maskp": st["maskp"], "onesb": st["ones_bf"], "onesf": st["ones_f32"]})
    res = run_bass_kernel_spmd(_nc("D"), in_maps, core_ids=CIDS)
    attn1T = [res.results[c]["attn1T"] for c in CIDS]

    in_maps = []
    for c in CIDS:
        seq, ptype, par, half, blocks, myblk = core_geometry(c)
        pe = (c // 2) * 2
        st = cores[c]
        sl = slice(par * 1024, (par + 1) * 1024)
        attn1_my = np.concatenate([attn1T[pe][:, sl], attn1T[pe + 1][:, sl]])
        off = 1024 * (par ^ ptype)
        in_maps.append({
            "attn1_my": np.ascontiguousarray(attn1_my),
            "x2myT": np.ascontiguousarray(x2f[c][:, off:off + 1024]),
            "wo1_t": st["wo1_t"], "wo1_b": st["wo1_b"],
            "fwcb": st["fwcb"], "fscb": st["fscb"],
            "onesb": st["ones_bf"], "onesf": st["ones_f32"]})
    res = run_bass_kernel_spmd(_nc("E"), in_maps, core_ids=CIDS)

    out = np.empty((B, S, E), dtype=np.float32)
    for c in CIDS:
        seq, ptype, par, half, blocks, myblk = core_geometry(c)
        out[seq, myblk * 1024:(myblk + 1) * 1024, :] = res.results[c]["outT"].T
    return out
